# version 16
# baseline (speedup 1.0000x reference)
import numpy as np
import concourse.bass as bass
import concourse.mybir as mybir
from contextlib import ExitStack

F32 = mybir.dt.float32
BF16 = mybir.dt.bfloat16
I8 = mybir.dt.int8
AF = mybir.ActivationFunctionType
ALU = mybir.AluOpType
AX = mybir.AxisListType

MAGIC = 12582912.0
D = 1024
DK = 64
NH = 16
FC = D // 128
INNER = 4096
IC = INNER // 128


class Sem:
    __slots__ = ("h", "n", "name")

    def __init__(self, h, name=""):
        self.h, self.n, self.name = h, 0, name

    def at(self):
        return (self, self.n)


class Plan:
    ENGS = ("sync", "tensor", "vector", "scalar", "gpsimd")

    def __init__(self, eng_sems):
        self.steps = {e: [] for e in self.ENGS}
        self.eng_sems = eng_sems

    def emit(self, eng, fn, waits=(), inc="auto", inc_amt=1):
        if inc == "auto":
            inc = self.eng_sems[eng]
        tok, incs = None, []
        if inc is not None:
            inc.n += inc_amt
            incs = [(inc, inc_amt)]
            tok = (inc, inc.n)
        self.steps[eng].append(([w for w in waits if w is not None], fn, incs))
        return tok

    def dma(self, eng, out, in_, waits=(), inc=None):
        assert inc is not None and inc != "auto"
        if not hasattr(self, "_last_dma"):
            self._last_dma = {}
        prev = self._last_dma.get(id(inc))
        tok = self.emit(eng, lambda e, o=out, i=in_: e.dma_start(out=o, i_=None) if False else e.dma_start(out=o, in_=i),
                        waits=list(waits) + ([prev] if prev else []),
                        inc=inc, inc_amt=16)
        self._last_dma[id(inc)] = tok
        return tok

    def barrier(self, dma_sems=()):
        toks = [s.at() for s in self.eng_sems.values()]
        toks += [s.at() for s in dma_sems]
        return [t for t in toks if t[1] > 0]

    def build(self, block):
        for eng_name in self.ENGS:
            steps = self.steps[eng_name]
            if not steps:
                continue

            def body(eng, steps=steps):
                seen = {}
                for waits, fn, incs in steps:
                    for s, v in waits:
                        if seen.get(id(s), -1) < v:
                            eng.wait_ge(s.h, v)
                            seen[id(s)] = v
                    ins = fn(eng)
                    for s, amt in incs:
                        assert ins is not None
                        ins.then_inc(s.h, amt)

            getattr(block, eng_name)(body)


class Arena:
    def __init__(self, handle, nbytes):
        self.t, self.nbytes = handle, nbytes

    def view(self, off, dtype, *dims):
        es_ = mybir.dt.size(dtype)
        assert off % es_ == 0
        n = 1
        for d in dims:
            n *= d
        assert off + n * es_ <= self.nbytes, (off, n * es_, self.nbytes)
        v = self.t.ap().bitcast(dtype)[:, off // es_: off // es_ + n]
        if len(dims) > 1:
            names = " ".join(f"d{i}" for i in range(len(dims)))
            v = v.rearrange(f"p ({names}) -> p {names}",
                            **{f"d{i}": dims[i] for i in range(len(dims))})
        return v


def build(nseq=2, S=2048, debug_outs=()):
    NTOK = nseq * S
    TT = NTOK // 128
    WA = min(512, NTOK)
    NWA = NTOK // WA
    QW = min(512, S)
    TW = S // QW
    TC = S // 128
    TCW = QW // 128
    SPT = S // 128

    nc = bass.Bass("TRN2", target_bir_lowering=False)

    x_h = nc.dram_tensor("x", [NTOK, D], F32, kind="ExternalInput").ap()
    wqkv_h = nc.dram_tensor("w_qkv", [3 * NH * DK, D], F32, kind="ExternalInput").ap()
    wo_h = nc.dram_tensor("w_o", [D, D], F32, kind="ExternalInput").ap()
    wg_h = nc.dram_tensor("w_gate", [INNER, D], F32, kind="ExternalInput").ap()
    wu_h = nc.dram_tensor("w_up", [INNER, D], F32, kind="ExternalInput").ap()
    wd_h = nc.dram_tensor("w_down", [D, INNER], F32, kind="ExternalInput").ap()
    out_h = nc.dram_tensor("out", [NTOK, D], F32, kind="ExternalOutput").ap()

    dbg = {}
    def DBG(name, shape, dtype=F32):
        if name in debug_outs:
            dbg[name] = nc.dram_tensor("dbg_" + name, shape, dtype,
                                       kind="ExternalOutput").ap()

    DBG("xn", [NTOK, D])

    DBG("qkv", [3, NH, DK, NTOK])
    DBG("h1", [NTOK, D])
    DBG("wq", [2, 128, FC * INNER], BF16)
    DBG("wdq", [IC, 128, D], BF16)
    DBG("hq", [TT, 128, INNER], BF16)
    DBG("acat", [nseq, 128, 8, S])
    DBG("xq", [NTOK, D], BF16)

    qkv_sp = nc.dram_tensor("qkv_sp", [3, NH, DK, NTOK], F32, kind="Internal").ap()
    h1_sp = nc.dram_tensor("h1_sp", [NTOK, D], F32, kind="Internal").ap()
    hq_sp = nc.dram_tensor("hq_sp", [TT, 128, INNER], BF16, kind="Internal").ap()
    wd_sp = nc.dram_tensor("wd_sp", [IC, 128, D], BF16, kind="Internal").ap()

    es = ExitStack()
    AR_BYTES = 201120
    with es:
        arena_t = es.enter_context(nc.sbuf_tensor("arena", [128, AR_BYTES], I8))
        A = Arena(arena_t, AR_BYTES)

        def T(name, shape, dtype=F32):
            return es.enter_context(nc.sbuf_tensor(name, shape, dtype))

        ident = T("ident", [128, 128])
        identb = T("identb", [128, 128], BF16)
        tri = T("tri", [128, 128])
        trib = T("trib", [128, 128], BF16)
        ones_row = T("ones_row", [1, 128])
        ones_colb = T("ones_colb", [128, 1], BF16)
        sel = T("sel", [NH, 8, 128])
        mean_acc = T("mean_acc", [128, 4])
        mean_red = T("mean_red", [128, 1])
        wsc = T("wsc", [1, 8])
        scb = T("scb", [128, 8])
        ln_t = T("ln_t", [128, 16])
        qs_t = T("qs_t", [128, 16])
        rs2 = T("rs2", [128, TT])
        sq_s = T("sq_s", [128, 1024])
        sqb_s = T("sqb_s", [128, 512], BF16)

        ps_a = es.enter_context(nc.psum_tensor("ps_a", [128, 512], F32))
        ps_b = es.enter_context(nc.psum_tensor("ps_b", [128, 512], F32))
        ps_trf = es.enter_context(nc.psum_tensor("ps_trf", [128, 512], F32))
        ps_trb = es.enter_context(nc.psum_tensor("ps_trb", [128, 1024], BF16))
        ps_sc = [es.enter_context(nc.psum_tensor(f"ps_sc{i}", [128, 512], F32))
                 for i in range(2)]
        ps_pv = es.enter_context(nc.psum_tensor("ps_pv", [128, 512], F32))
        ps_x = es.enter_context(nc.psum_tensor("ps_x", [128, 512], F32))

        def S_(name):
            return Sem(es.enter_context(nc.semaphore(name)), name)

        eng_sems = {e: S_("e_" + e) for e in Plan.ENGS}
        p = Plan(eng_sems)
        sm = {n: S_(n) for n in [
            "ldw", "ldw2", "ldxa0", "ldxa1", "stqkv", "stqkv1", "stqkv2", "ldq", "ldk", "ldv",
            "ldxb0", "ldxb1", "sth10", "sth11", "ldwf0", "ldwf1", "stwd",
            "asm", "asm1", "asm2", "sthq", "ldh0", "ldh1", "ldr0", "ldr1", "ldwd",
            "stout0", "stout1", "lddbg"]}

        oA = 0
        wqkvT_hi = A.view(oA, BF16, FC, 3 * NH * DK)
        wqkvT_lo = A.view(oA + FC * 3 * NH * DK * 2, BF16, FC, 3 * NH * DK)
        oA += FC * 3 * NH * DK * 4
        o_stg = oA
        stg = A.view(oA, F32, 24, WA); oA += 24 * WA * 4
        xa = [A.view(oA + i * D * 4, F32, D) for i in range(4)]
        oA += 4 * D * 4
        xnT_hi = [A.view(oA, BF16, FC, WA), A.view(oA + FC * WA * 4, BF16, FC, WA)]
        xnT_lo = [A.view(oA + FC * WA * 2, BF16, FC, WA),
                  A.view(oA + FC * WA * 6, BF16, FC, WA)]
        oA += 2 * FC * WA * 4
        mbuf = [A.view(oA, F32, 512), A.view(oA + 512 * 4, F32, 512)]
        oA += 2 * 512 * 4
        assert oA <= AR_BYTES, oA
        wld = [A.view(o_stg, F32, 2, D), A.view(o_stg + 2 * D * 4, F32, 2, D)]

        oB = 0
        qkvb = A.view(oB, F32, 3, S); oB += 3 * S * 4
        vpr = A.view(oB, F32, TC, 65); oB += TC * 65 * 4
        o_expT = oB
        expT = A.view(oB, F32, TC, QW); oB += TC * QW * 4
        ubuf = A.view(o_expT, BF16, TC, QW)
        stgh = A.view(oB, F32, S); oB += S * 4
        sums = A.view(oB, F32, S); oB += S * 4
        acat = A.view(oB, F32, 8, S); oB += 8 * S * 4
        woT = A.view(oB, F32, FC, D); oB += FC * D * 4
        xb = [A.view(oB, F32, D), A.view(oB + D * 4, F32, D)]
        oB += 2 * D * 4
        h1b = [A.view(oB, F32, D), A.view(oB + D * 4, F32, D)]
        oB += 2 * D * 4
        qtb = A.view(oB, BF16, S); oB += S * 2
        ktb = A.view(oB, BF16, S); oB += S * 2
        assert oB <= AR_BYTES, oB

        oC = 0
        wgTq = A.view(oC, BF16, FC, INNER); oC += FC * INNER * 2
        wuTq = A.view(oC, BF16, FC, INNER); oC += FC * INNER * 2
        oB2 = oC
        wfl = [A.view(oB2, F32, INNER), A.view(oB2 + INNER * 4, F32, INNER)]
        oB2 += 2 * INNER * 4
        wtmp = A.view(oB2, F32, INNER); oB2 += INNER * 4
        wqn = A.view(oB2, BF16, INNER); oB2 += INNER * 2
        wdst = A.view(oB2, BF16, 8, 128); oB2 += 8 * 128 * 2
        assert oB2 <= AR_BYTES, oB2

        oC1 = oC
        h1c = [A.view(oC1, F32, D), A.view(oC1 + D * 4, F32, D)]
        oC1 += 2 * D * 4
        xn2 = [A.view(oC1, F32, D), A.view(oC1 + D * 4, F32, D)]
        oC1 += 2 * D * 4
        xqb = [A.view(oC1, BF16, D), A.view(oC1 + D * 2, BF16, D)]
        oC1 += 2 * D * 2
        xqT = [A.view(oC1, BF16, FC, 128), A.view(oC1 + FC * 128 * 2, BF16, FC, 128)]
        oC1 += 2 * FC * 128 * 2
        hsil = A.view(oC1, F32, INNER); oC1 += INNER * 4
        htmp = A.view(oC1, F32, INNER); oC1 += INNER * 4
        hqt = A.view(oC1, BF16, INNER); oC1 += INNER * 2
        assert oC1 <= AR_BYTES, oC1

        oC2 = oC
        hqb = [A.view(oC2, BF16, INNER), A.view(oC2 + INNER * 2, BF16, INNER)]
        oC2 += 2 * INNER * 2
        hqTs = [A.view(oC2, BF16, IC, 128), A.view(oC2 + IC * 128 * 2, BF16, IC, 128)]
        oC2 += 2 * IC * 128 * 2
        h1r = [A.view(oC2, F32, D), A.view(oC2 + D * 4, F32, D)]
        oC2 += 2 * D * 4
        outsb = [A.view(oC2, F32, D), A.view(oC2 + D * 4, F32, D)]
        oC2 += 2 * D * 4
        assert oC2 <= AR_BYTES, oC2
        wdTq = A.view(0, BF16, IC, D)

        def ln_apply(src, dst, waits, par=0):
            o = 8 * par
            lt = lambda i: ln_t[:, o + i:o + i + 1]
            t1 = p.emit("vector", lambda e: e.reduce_sum(lt(0), src, axis=AX.X),
                        waits=waits)
            t2 = p.emit("scalar", lambda e: e.activation(
                sq_s[:, 0:D], src, AF.Square, accum_out=lt(1)), waits=waits)
            t3 = p.emit("vector", lambda e: e.tensor_scalar(
                lt(2), lt(0), 1.0 / D, None, ALU.mult), waits=[t1])
            t4 = p.emit("vector", lambda e: e.tensor_scalar(
                lt(3), lt(1), 1.0 / D, None, ALU.mult), waits=[t2])
            t5 = p.emit("vector", lambda e: e.tensor_scalar(
                lt(4), lt(2), lt(2), None, ALU.mult), waits=[t3])
            t6 = p.emit("vector", lambda e: e.scalar_tensor_tensor(
                lt(5), lt(3), 1e-5, lt(4), ALU.add, ALU.subtract), waits=[t4, t5])
            t7 = p.emit("vector", lambda e: e.reciprocal(lt(6), lt(5)), waits=[t6])
            t8 = p.emit("scalar", lambda e: e.activation(lt(7), lt(6), AF.Sqrt),
                        waits=[t7])
            t9 = p.emit("vector", lambda e: e.tensor_scalar(
                lt(0), lt(2), lt(7), -1.0, ALU.mult, ALU.mult), waits=[t8])
            return p.emit("scalar", lambda e: e.activation(
                dst, src, AF.Identity, bias=lt(0), scale=lt(7)), waits=[t9])

        t0 = p.emit("gpsimd", lambda e: e.memset(tri[:], 1.0))
        t0 = p.emit("gpsimd", lambda e: e.affine_select(
            tri[:], tri[:], [[1, 128]], ALU.is_ge, 0.0, channel_multiplier=-1),
            waits=[t0])
        t0 = p.emit("gpsimd", lambda e: e.iota(
            sq_s[:, 0:128], [[0, 128]], channel_multiplier=1,
            allow_small_or_imprecise_dtypes=True), waits=[t0])
        t0 = p.emit("gpsimd", lambda e: e.iota(
            sq_s[:, 128:256], [[1, 128]], channel_multiplier=0,
            allow_small_or_imprecise_dtypes=True), waits=[t0])
        t_id = p.emit("vector", lambda e: e.tensor_tensor(
            ident[:], sq_s[:, 0:128], sq_s[:, 128:256], ALU.is_equal), waits=[t0])
        t_idb = p.emit("vector", lambda e: e.tensor_copy(identb[:], ident[:]),
                       waits=[t_id])
        t0 = p.emit("gpsimd", lambda e: e.memset(ones_row[:], 1.0), waits=[t0])
        t0 = p.emit("gpsimd", lambda e: e.memset(ones_colb[:], 1.0), waits=[t0])
        t0 = p.emit("gpsimd", lambda e: e.iota(
            sq_s[0:16, 0:1024], [[1, 16], [0, 64]], channel_multiplier=0,
            allow_small_or_imprecise_dtypes=True), waits=[t0, t_id])
        t0 = p.emit("gpsimd", lambda e: e.iota(
            sel.ap().rearrange("p a b -> p (a b)"), [[0, 1024]],
            channel_multiplier=1, allow_small_or_imprecise_dtypes=True),
            waits=[t0])
        t0 = p.emit("vector", lambda e: e.tensor_tensor(
            sel.ap().rearrange("p a b -> p (a b)"),
            sel.ap().rearrange("p a b -> p (a b)"),
            sq_s[0:16, 0:1024], ALU.is_equal), waits=[t0, t_idb])
        t0 = p.emit("gpsimd", lambda e: e.memset(mean_acc[:], 0.0), waits=[t0])
        consts = [t0, t_id, t_idb]

        def prep_wT(dst_view, src_h, nblk, waits0, bufs, dst_lo=None):
            toks = []
            for bp in range(nblk // 2):
                b = bp % 2
                lw = p.dma("sync", bufs[b][:, :, :],
                           src_h[bp * 256:(bp + 1) * 256, :]
                           .rearrange("(a p) d -> p a d", p=128),
                           waits=list(waits0) + ([toks[-4]] if len(toks) >= 4 else []),
                           inc=sm["ldw" if b == 0 else "ldw2"])
                for half in range(2):
                    blk = bp * 2 + half
                    for g in range(2):
                        trs = None
                        for j in range(4):
                            fcx = g * 4 + j
                            trs = p.emit("tensor", lambda e, b=b, half=half, fcx=fcx,
                                         j=j: e.transpose(
                                ps_trf[:, j * 128:(j + 1) * 128],
                                bufs[b][:, half, fcx * 128:(fcx + 1) * 128],
                                ident[:]),
                                waits=[lw] + ([toks[-1]] if toks else list(waits0)))
                        hi = p.emit("vector", lambda e, blk=blk, g=g: e.tensor_copy(
                            dst_view[:, g * 4:(g + 1) * 4, blk * 128:(blk + 1) * 128],
                            ps_trf[:, 0:512].rearrange("p (f c) -> p f c", f=4)),
                            waits=[trs])
                        if dst_lo is not None:
                            hi = p.emit("vector", lambda e, blk=blk, g=g:
                                        e.scalar_tensor_tensor(
                                dst_lo[:, g * 4:(g + 1) * 4,
                                       blk * 128:(blk + 1) * 128],
                                ps_trf[:, 0:512].rearrange("p (f c) -> p f c", f=4),
                                1.0,
                                dst_view[:, g * 4:(g + 1) * 4,
                                         blk * 128:(blk + 1) * 128],
                                ALU.mult, ALU.subtract), waits=[hi])
                        toks.append(hi)
            return toks[-1]

        wqkvT_done = prep_wT(wqkvT_hi, wqkv_h, 3 * NH * DK // 128, consts, wld,
                             dst_lo=wqkvT_lo)

        mean_toks = []
        NMC = INNER * D // (128 * 512)
        _mean_jobs = []
        for m3, wsrc in enumerate([wg_h, wu_h, wd_h]):
            wflat = wsrc.rearrange("a b -> (a b)").rearrange("(r d) -> r d", d=512)
            for ci in range(NMC):
                _mean_jobs.append((m3, wflat, ci))

        def emit_mean(n):
            while n > 0 and _mean_jobs:
                m3, wflat, ci = _mean_jobs.pop(0)
                b = len(mean_toks) % 2
                lw = p.dma("scalar", mbuf[b][:, :], wflat[ci * 128:(ci + 1) * 128, :],
                           waits=list(consts) +
                                 ([mean_toks[-2]] if len(mean_toks) >= 2 else []),
                           inc=sm["ldwf0" if b == 0 else "ldwf1"])
                r1 = p.emit("vector", lambda e, b=b: e.reduce_sum(
                    mean_red[:, 0:1], mbuf[b][:, :], axis=AX.X,
                    apply_absolute_value=True),
                    waits=[lw] + ([mean_toks[-1]] if mean_toks else []))
                r2 = p.emit("vector", lambda e, m3=m3: e.tensor_tensor(
                    mean_acc[:, m3:m3 + 1], mean_acc[:, m3:m3 + 1],
                    mean_red[:, 0:1], ALU.add), waits=[r1])
                mean_toks.append(r2)
                n -= 1

        a_buf_free = {}
        stage_free = [wqkvT_done]
        tr_toks = []
        blk_ev = []
        a_tr = {}
        a_dbg = {}

        def a_pre(w):
            tile_last = None
            tls = []
            for i in range(WA // 128):
                gi = w * (WA // 128) + i
                bx = gi % 4
                t0g = w * WA + i * 128
                lx = p.dma("sync", xa[bx][:, :], x_h[t0g:t0g + 128, :],
                           waits=(list(consts) if gi < 4
                                  else a_buf_free[gi - 4]),
                           inc=sm["ldxa0" if bx % 2 == 0 else "ldxa1"])
                srcv = xa[bx][:, :]
                tln = ln_apply(srcv, srcv, [lx] + ([tile_last] if tile_last else []),
                               par=w % 2)
                dbg_tok = None
                if "xn" in dbg:
                    dbg_tok = p.dma("scalar",
                                    dbg["xn"][t0g:t0g + 128, :],
                                    srcv, waits=[tln], inc=sm["lddbg"])
                a_dbg[gi] = dbg_tok
                tile_last = tln
                tls.append(tln)
            return tls

        def a_tr_emit(w, tls):
            for i in range(WA // 128):
                gi = w * (WA // 128) + i
                srcv = xa[gi % 4][:, :]
                for g in range(2):
                    trs = None
                    for j in range(4):
                        fcx = g * 4 + j
                        trs = p.emit("tensor", lambda e, srcv=srcv, fcx=fcx, j=j:
                                     e.transpose(ps_trf[:, j * 128:(j + 1) * 128],
                                                 srcv[:, fcx * 128:(fcx + 1) * 128],
                                                 ident[:]),
                                     waits=[tls[i]] +
                                           ([tr_toks[-1]] if tr_toks else [wqkvT_done]))
                    ev = p.emit("vector", lambda e, w=w, i=i, g=g: e.tensor_copy(
                        xnT_hi[w % 2][:, g * 4:(g + 1) * 4, i * 128:(i + 1) * 128],
                        ps_trf[:, 0:512].rearrange("p (f c) -> p f c", f=4)),
                        waits=[trs])
                    ev = p.emit("vector", lambda e, w=w, i=i, g=g:
                                e.scalar_tensor_tensor(
                        xnT_lo[w % 2][:, g * 4:(g + 1) * 4, i * 128:(i + 1) * 128],
                        ps_trf[:, 0:512].rearrange("p (f c) -> p f c", f=4), 1.0,
                        xnT_hi[w % 2][:, g * 4:(g + 1) * 4, i * 128:(i + 1) * 128],
                        ALU.mult, ALU.subtract), waits=[ev])
                    tr_toks.append(ev)
                a_buf_free[gi] = [tr_toks[-1]] + ([a_dbg[gi]] if a_dbg.get(gi) else [])
            a_tr[w] = tr_toks[-1]

        def a_main(w):
            nonlocal_stage = None
            for blk in range(24):
                psq = ps_a if blk % 2 == 0 else ps_b
                mm = None
                passes = [(wqkvT_hi, xnT_hi), (wqkvT_lo, xnT_hi), (wqkvT_hi, xnT_lo)]
                for pi, (wv, xv) in enumerate(passes):
                    for fc in range(FC):
                        w0 = []
                        if pi == 0 and fc == 0:
                            w0 = [a_tr[w], wqkvT_done]
                            if len(blk_ev) >= 2:
                                w0.append(blk_ev[-2])
                        mm = p.emit("tensor", lambda e, blk=blk, fc=fc, psq=psq, w=w,
                                    wv=wv, xv=xv, pi=pi: e.matmul(
                            psq[:, 0:WA], wv[:, fc, blk * 128:(blk + 1) * 128],
                            xv[w % 2][:, fc, 0:WA], start=(pi == 0 and fc == 0),
                            stop=(pi == 2 and fc == FC - 1)), waits=w0)
                qs = 0.125 if blk < 8 else 1.0
                evs = p.emit("scalar", lambda e, blk=blk, qs=qs, psq=psq: e.activation(
                    stg[:, blk, :], psq[:, 0:WA], AF.Copy, scale=qs),
                    waits=[mm] + stage_free)
                blk_ev.append(evs)
            sp = []
            for q3 in range(3):
                for ph in range(2):
                    srcv = stg[ph * 64:(ph + 1) * 64, q3 * 8:(q3 + 1) * 8, :]
                    dst = qkv_sp[q3].rearrange("h k t -> (h k) t") \
                        .rearrange("(bb two k) t -> bb two k t", two=2, k=64) \
                        [:, ph, :, w * WA:(w + 1) * WA].transpose([1, 0, 2])
                    sp.append(p.dma("sync", dst, srcv, waits=[blk_ev[-1]],
                                    inc=sm[["stqkv", "stqkv1", "stqkv2"][len(sp) % 3]]))
            return sp

        tls = a_pre(0)
        a_tr_emit(0, tls)
        for w in range(NWA):
            if w + 1 < NWA:
                tls = a_pre(w + 1)
            sp = a_main(w)
            stage_free = sp
            if w + 1 < NWA:
                a_tr_emit(w + 1, tls)
            emit_mean((len(_mean_jobs) + (NWA - 1 - w)) // max(1, NWA - w))

        emit_mean(len(_mean_jobs))
        mean_done = mean_toks[-1]
        a_done = p.barrier(dma_sems=[sm["stqkv"], sm["stqkv1"], sm["stqkv2"],
                                     sm["ldxa0"], sm["ldxa1"],
                                     sm["ldw"], sm["ldw2"], sm["ldwf0"], sm["ldwf1"],
                                     sm["lddbg"]])

        if "qkv" in dbg:
            dq = p.dma("sync", dbg["qkv"].rearrange("a h k t -> (a h k) t"),
                       qkv_sp.rearrange("a h k t -> (a h k) t"),
                       waits=a_done, inc=sm["lddbg"])
            a_done = a_done + [dq]

        wldB = [A.view(0, F32, 2, D), A.view(2 * D * 4, F32, 2, D)]
        woT_done = prep_wT(woT, wo_h, FC, a_done, wldB)

        vones = p.emit("gpsimd", lambda e: e.memset(vpr[:, :, 64:65], 1.0),
                       waits=a_done + [woT_done])
        kones = p.emit("vector", lambda e: e.memset(qkvb[64:65, 1, 0:S], 1.0),
                       waits=[woT_done])
        tribm = p.emit("vector", lambda e: e.tensor_copy(trib[:], tri[:]),
                       waits=a_done)
        negM_done = kones

        prev_hs = [negM_done, vones, woT_done]
        for s in range(nseq):
            seq_asm = []
            for h in range(NH):
                lq = p.dma("sync", qkvb[0:64, 0, :], qkv_sp[0, h, :, s * S:(s + 1) * S],
                           waits=prev_hs, inc=sm["ldq"])
                lk = p.dma("sync", qkvb[0:64, 1, :], qkv_sp[1, h, :, s * S:(s + 1) * S],
                           waits=prev_hs, inc=sm["ldk"])
                lv = p.dma("sync", qkvb[0:64, 2, :], qkv_sp[2, h, :, s * S:(s + 1) * S],
                           waits=prev_hs, inc=sm["ldv"])
                cq = p.emit("vector", lambda e: e.tensor_copy(
                    qtb[0:64, :], qkvb[0:64, 0, :]), waits=[lq] + prev_hs)
                ck = p.emit("vector", lambda e: e.tensor_copy(
                    ktb[0:64, :], qkvb[0:64, 1, :]), waits=[lk, cq])
                vt_toks = []
                for tg in range((TC + 7) // 8):
                    n_in_g = min(8, TC - tg * 8)
                    trs = None
                    for j in range(n_in_g):
                        tcx = tg * 8 + j
                        trs = p.emit("tensor", lambda e, tcx=tcx, j=j: e.transpose(
                            ps_trf[:, j * 64:(j + 1) * 64],
                            qkvb[0:64, 2, tcx * 128:(tcx + 1) * 128],
                            ident[0:64, 0:64]),
                            waits=[lv] + ([vt_toks[-1]] if vt_toks else prev_hs))
                    ev = p.emit("vector", lambda e, tg=tg, n=n_in_g: e.tensor_copy(
                        vpr[:, tg * 8:tg * 8 + n, 0:64],
                        ps_trf[:, 0:n * 64].rearrange("p (a k) -> p a k", k=64)),
                        waits=[trs])
                    vt_toks.append(ev)
                v_done = vt_toks[-1]

                exp_tok = {}
                sc_prev = {}
                hs_pv = []
                uprev = None
                for w in range(TW):
                    ntc = min(TC, TCW * (w + 1))
                    umm = None
                    utoks = []
                    for Tt in range(ntc):
                        diag = Tt >= TCW * w
                        q_off = (Tt - TCW * w) * 128 if diag else 0
                        psc = (ps_sc + [ps_pv])[Tt % 3]
                        wx = [cq, ck]
                        if (Tt % 3) in sc_prev:
                            wx.append(sc_prev[Tt % 3])
                        smm = p.emit("tensor", lambda e, Tt=Tt, psc=psc, w=w:
                                     e.matmul(
                            psc[:, 0:QW], ktb[0:64, Tt * 128:(Tt + 1) * 128],
                            qtb[0:64, w * QW: w * QW + QW],
                            start=True, stop=True), waits=wx)
                        wu = [smm]
                        if uprev:
                            wu.append(uprev)
                        elif prev_hs:
                            wu.extend(prev_hs)
                        if w > 0 and Tt == 0 and hs_pv:
                            wu.append(hs_pv[-1])
                        uev = p.emit("scalar", lambda e, psc=psc, Tt=Tt: e.activation(
                            ubuf[:, Tt, :], psc[:, 0:QW], AF.Exp, scale=0.125),
                            waits=wu)
                        sc_prev[Tt % 3] = uev
                        uprev = uev
                        utok = uev
                        if diag:
                            if q_off > 0:
                                utok = p.emit("vector", lambda e, Tt=Tt, q_off=q_off:
                                              e.memset(ubuf[:, Tt, 0:q_off], 0.0),
                                              waits=[uev])
                            utok = p.emit("vector", lambda e, Tt=Tt, q_off=q_off:
                                          e.tensor_tensor(
                                ubuf[:, Tt, q_off:q_off + 128],
                                ubuf[:, Tt, q_off:q_off + 128], trib[:], ALU.mult),
                                waits=[utok, tribm])
                        utoks.append(utok)
                    for Tt in range(ntc):
                        umm = p.emit("tensor", lambda e, Tt=Tt, ntc=ntc: e.matmul(
                            ps_x[0:1, 0:QW], ones_colb[:, :], ubuf[:, Tt, :],
                            start=(Tt == 0), stop=(Tt == ntc - 1)),
                            waits=[utoks[Tt]] + ([umm] if umm else []))
                    lnu = p.emit("scalar", lambda e: e.activation(
                        sq_s[0:1, 0:QW], ps_x[0:1, 0:QW], AF.Ln), waits=[umm])
                    mrow = p.emit("vector", lambda e, w=w: e.tensor_scalar(
                        qkvb[64:65, 0, w * QW:(w + 1) * QW], sq_s[0:1, 0:QW],
                        -8.0, None, ALU.mult), waits=[lnu, negM_done])
                    for Tt in range(ntc):
                        diag = Tt >= TCW * w
                        q_off = (Tt - TCW * w) * 128 if diag else 0
                        nq = QW - q_off
                        psc = (ps_sc + [ps_pv])[Tt % 3]
                        wx = [lq, lk, mrow]
                        if (Tt % 3) in sc_prev:
                            wx.append(sc_prev[Tt % 3])
                        mmsc = p.emit("tensor", lambda e, Tt=Tt, q_off=q_off, nq=nq,
                                      psc=psc, w=w: e.matmul(
                            psc[:, q_off:q_off + nq],
                            qkvb[0:65, 1, Tt * 128:(Tt + 1) * 128],
                            qkvb[0:65, 0, w * QW + q_off: w * QW + q_off + nq],
                            start=True, stop=True), waits=wx)
                        wexp = [mmsc]
                        if hs_pv:
                            wexp.append(hs_pv[-1])
                        elif prev_hs:
                            wexp.extend(prev_hs)
                        msz = None
                        if diag and q_off > 0:
                            msz = p.emit("vector", lambda e, Tt=Tt, q_off=q_off:
                                         e.memset(expT[:, Tt, 0:q_off], 0.0),
                                         waits=list(wexp[1:]))
                        ex = p.emit("scalar", lambda e, Tt=Tt, q_off=q_off, nq=nq,
                                    psc=psc: e.activation(
                            expT[:, Tt, q_off:q_off + nq], psc[:, q_off:q_off + nq],
                            AF.Exp), waits=wexp)
                        sc_prev[Tt % 3] = ex
                        if diag:
                            ex = p.emit("vector", lambda e, Tt=Tt, q_off=q_off:
                                        e.tensor_tensor(
                                expT[:, Tt, q_off:q_off + 128],
                                expT[:, Tt, q_off:q_off + 128], tri[:], ALU.mult),
                                waits=[ex] + ([msz] if msz else []))
                        exp_tok[(w, Tt)] = ex
                    mmpv = None
                    for Tt in range(ntc):
                        wx = [exp_tok[(w, Tt)], v_done]
                        if mmpv is not None:
                            wx = [mmpv, exp_tok[(w, Tt)]]
                        elif hs_pv:
                            wx.append(hs_pv[-1])
                        mmpv = p.emit("tensor", lambda e, Tt=Tt, ntc=ntc: e.matmul(
                            ps_a[0:65, 0:QW], vpr[:, Tt, :], expT[:, Tt, 0:QW],
                            start=(Tt == 0), stop=(Tt == ntc - 1)), waits=wx)
                    evpv = p.emit("vector", lambda e, w=w: e.tensor_copy(
                        stgh[0:65, w * QW:(w + 1) * QW], ps_a[0:65, 0:QW]),
                        waits=[mmpv])
                    hs_pv.append(evpv)
                a1 = p.dma("scalar", acat[(h % 2) * 64:(h % 2) * 64 + 64, h // 2, :],
                           stgh[0:64, :], waits=[hs_pv[-1]],
                           inc=sm[["asm", "asm1"][h % 2]])
                a2 = p.dma("scalar", sums[h:h + 1, 0:S], stgh[64:65, :],
                           waits=[hs_pv[-1], a1], inc=sm["asm2"])
                seq_asm.append(a2)
                prev_hs = [hs_pv[-1], a2]
            rc = p.emit("vector", lambda e: e.reciprocal(
                sums[0:NH, 0:S], sums[0:NH, 0:S]), waits=seq_asm)
            nrm_tok = None
            for pb in range(8):
                for w in range(TW):
                    bc = p.emit("tensor", lambda e, pb=pb, w=w: e.matmul(
                        ps_b[:, 0:QW], sel[:, pb, :],
                        sums[0:NH, w * QW:(w + 1) * QW],
                        start=True, stop=True),
                        waits=[rc] + ([nrm_tok] if nrm_tok else []))
                    nrm_tok = p.emit("vector", lambda e, pb=pb, w=w: e.tensor_tensor(
                        acat[:, pb, w * QW:(w + 1) * QW],
                        acat[:, pb, w * QW:(w + 1) * QW], ps_b[:, 0:QW], ALU.mult),
                        waits=[bc])
            n_done = nrm_tok
            if "acat" in dbg:
                n_done = p.dma("scalar", dbg["acat"][s].rearrange("p a t -> p (a t)"),
                               acat[:, :, :].rearrange("p a t -> p (a t)"),
                               waits=[n_done], inc=sm["lddbg"])
            ox_st = {}
            ox_ev = None
            for tt in range(SPT):
                bx = tt % 2
                lxr = p.dma("sync", xb[bx][:, :],
                            x_h[s * S + tt * 128: s * S + (tt + 1) * 128, :],
                            waits=[n_done] + ([ox_st.get(bx)] if ox_st.get(bx) else []),
                            inc=sm["ldxb0" if bx == 0 else "ldxb1"])
                ev_h1 = []
                for mw in range(2):
                    mm = None
                    for pb in range(8):
                        w0 = []
                        if pb == 0:
                            w0 = [n_done, woT_done]
                            if ev_h1:
                                w0.append(ev_h1[-1])
                            elif ox_ev is not None:
                                w0.append(ox_ev)
                        mm = p.emit("tensor", lambda e, pb=pb, tt=tt, mw=mw: e.matmul(
                            ps_a[:, 0:512], acat[:, pb, tt * 128:(tt + 1) * 128],
                            woT[:, pb, mw * 512:(mw + 1) * 512],
                            start=(pb == 0), stop=(pb == 7)), waits=w0)
                    ev = p.emit("vector", lambda e, mw=mw, bx=bx: e.scalar_tensor_tensor(
                        h1b[bx][:, mw * 512:(mw + 1) * 512], ps_a[:, 0:512], 1.0,
                        xb[bx][:, mw * 512:(mw + 1) * 512], ALU.mult, ALU.add),
                        waits=[mm, lxr] + ([ox_st.get(bx)] if ox_st.get(bx) else []))
                    ev_h1.append(ev)
                st = p.dma("sync", h1_sp[s * S + tt * 128: s * S + (tt + 1) * 128, :],
                           h1b[bx][:, :], waits=[ev_h1[-1]],
                           inc=sm["sth10" if bx == 0 else "sth11"])
                if "h1" in dbg:
                    st = p.dma("scalar",
                               dbg["h1"][s * S + tt * 128: s * S + (tt + 1) * 128, :],
                               h1b[bx][:, :], waits=[ev_h1[-1], st], inc=sm["lddbg"])
                ox_st[bx] = st
                ox_ev = ev_h1[-1]
            prev_hs = [n_done, ox_ev] + [v for v in ox_st.values()]

        b_done = p.barrier(dma_sems=[sm["sth10"], sm["sth11"], sm["ldxb0"],
                                     sm["ldxb1"], sm["asm"], sm["asm1"], sm["asm2"],
                                     sm["ldq"], sm["ldk"],
                                     sm["ldv"], sm["ldw"], sm["ldw2"], sm["lddbg"]])

        t = p.emit("tensor", lambda e: e.matmul(
            ps_x[0:1, 0:4], ones_colb[:, :], mean_acc[:, 0:4].bitcast(F32)
            if False else mean_acc[:, 0:4], start=True, stop=True),
            waits=b_done + [mean_done]) if False else None
        ones_colf = tri[:, 127:128]
        t = p.emit("tensor", lambda e: e.matmul(
            ps_x[0:1, 0:4], ones_colf, mean_acc[:, 0:4], start=True, stop=True),
            waits=b_done + [mean_done])
        t = p.emit("vector", lambda e: e.tensor_copy(wsc[0:1, 0:4], ps_x[0:1, 0:4]),
                   waits=[t])
        t = p.emit("vector", lambda e: e.tensor_scalar(
            wsc[0:1, 0:3], wsc[0:1, 0:3], 1.0 / (INNER * D), 1e-5, ALU.mult,
            ALU.max), waits=[t])
        t = p.emit("vector", lambda e: e.reciprocal(wsc[0:1, 3:6], wsc[0:1, 0:3]),
                   waits=[t])
        t = p.emit("vector", lambda e: e.reciprocal(wsc[0:1, 6:8], wsc[0:1, 3:5]),
                   waits=[t])
        t = p.emit("vector", lambda e: e.reciprocal(wsc[0:1, 2:3], wsc[0:1, 5:6]),
                   waits=[t])
        bc = p.emit("tensor", lambda e: e.matmul(
            ps_x[0:128, 0:8], ones_row[:], wsc[0:1, 0:8], start=True, stop=True),
            waits=[t])
        scb_done = p.emit("vector", lambda e: e.tensor_copy(scb[:], ps_x[0:128, 0:8]),
                          waits=[bc])

        def quant_chunk(ldtok, src, width, swcol, prev):
            q1 = p.emit("vector", lambda e, src=src, c=swcol, wd=width: e.tensor_scalar(
                wtmp[:, 0:wd], src, scb[:, c:c + 1], MAGIC, ALU.mult, ALU.add),
                waits=[ldtok, scb_done] + prev)
            q2 = p.emit("vector", lambda e, wd=width: e.tensor_scalar(
                wtmp[:, 0:wd], wtmp[:, 0:wd], MAGIC, None, ALU.subtract), waits=[q1])
            return p.emit("vector", lambda e, wd=width: e.tensor_scalar(
                wqn[:, 0:wd], wtmp[:, 0:wd], 1.0, -1.0, ALU.min, ALU.max), waits=[q2])

        wq_done = []
        for m3, (wsrc, dstv) in enumerate([(wg_h, wgTq), (wu_h, wuTq)]):
            for c4 in range(IC // 4):
                gi = m3 * (IC // 4) + c4
                b = gi % 2
                lw = p.dma("scalar", wfl[b][:, 0:INNER]
                           .rearrange("p (a d) -> p a d", a=4),
                           wsrc[c4 * 512:(c4 + 1) * 512, :]
                           .rearrange("(a p) d -> p a d", p=128),
                           waits=b_done + ([wq_done[-8]] if len(wq_done) >= 8 else []),
                           inc=sm["ldwf0" if b == 0 else "ldwf1"])
                qz = quant_chunk(lw, wfl[b][:, 0:INNER], INNER, 3 + m3,
                                 [wq_done[-1]] if wq_done else [])
                for half in range(4):
                    ci = c4 * 4 + half
                    trs = None
                    for fcx in range(FC):
                        trs = p.emit("tensor", lambda e, half=half, fcx=fcx:
                                     e.transpose(
                            ps_trb[:, fcx * 128:(fcx + 1) * 128],
                            wqn[:, half * D + fcx * 128: half * D + (fcx + 1) * 128],
                            identb[:]),
                            waits=[qz] + ([wq_done[-1]] if wq_done else b_done))
                    ev = p.emit("vector", lambda e, dstv=dstv, ci=ci: e.tensor_copy(
                        dstv[:, :, ci * 128:(ci + 1) * 128],
                        ps_trb[:, 0:1024].rearrange("p (f c) -> p f c", f=FC)),
                        waits=[trs])
                    wq_done.append(ev)
        if "wq" in dbg:
            for m3, dstv in enumerate([wgTq, wuTq]):
                p.dma("scalar", dbg["wq"][m3],
                      dstv[:, :, :].rearrange("p f c -> p (f c)"),
                      waits=[wq_done[-1]], inc=sm["lddbg"])

        wd_toks = []
        for cj in range(D // 128):
            b = cj % 2
            lw = p.dma("scalar", wfl[b][:, :], wd_h[cj * 128:(cj + 1) * 128, :],
                       waits=b_done + [wq_done[-1]] +
                             ([wd_toks[-3]] if len(wd_toks) >= 3 else []),
                       inc=sm["ldwf0" if b == 0 else "ldwf1"])
            qz = quant_chunk(lw, wfl[b][:, :], INNER, 5,
                             [wd_toks[-1]] if wd_toks else [wq_done[-1]])
            for ig in range(IC // 8):
                trs = None
                for j in range(8):
                    icx = ig * 8 + j
                    trs = p.emit("tensor", lambda e, icx=icx, j=j: e.transpose(
                        ps_trb[:, j * 128:(j + 1) * 128],
                        wqn[:, icx * 128:(icx + 1) * 128], identb[:]),
                        waits=[qz] + ([wd_toks[-1]] if wd_toks else [wq_done[-1]]))
                ev = p.emit("vector", lambda e: e.tensor_copy(
                    wdst[:, :, :].rearrange("p a c -> p (a c)"), ps_trb[:, 0:1024]),
                    waits=[trs] + ([wd_toks[-1]] if wd_toks else []))
                st = p.dma("scalar",
                           wd_sp[ig * 8:(ig + 1) * 8, :, cj * 128:(cj + 1) * 128]
                           .rearrange("a p c -> p a c"), wdst[:, :, :],
                           waits=[ev], inc=sm["stwd"])
                wd_toks.append(st)
        if "wdq" in dbg:
            p.dma("scalar", dbg["wdq"].rearrange("a p c -> (a p) c"),
                  wd_sp.rearrange("a p c -> (a p) c"), waits=[wd_toks[-1]],
                  inc=sm["lddbg"])

        b2_done = p.barrier(dma_sems=[sm["ldwf0"], sm["ldwf1"], sm["stwd"],
                                      sm["lddbg"]])

        c1_last = None
        c1_buf = {}
        c1_pre_toks = {}
        c1_cmb_last = {}
        LNW = 8

        def c1_pre(tt, extra):
            b = tt % 2
            qo = 8 * b
            lh = p.dma("sync", h1c[b][:, :], h1_sp[tt * 128:(tt + 1) * 128, :],
                       waits=list(extra) + ([c1_buf.get(b)] if c1_buf.get(b) else []),
                       inc=sm["ldh0" if b == 0 else "ldh1"])
            tln = ln_apply(h1c[b][:, :], xn2[b][:, :], [lh], b)
            t1 = p.emit("vector", lambda e, b=b: e.reduce_max(
                qs_t[:, qo + 0:qo + 1], xn2[b][:, :], axis=AX.X,
                apply_absolute_value=True), waits=[tln])
            t2 = p.emit("vector", lambda e: e.tensor_scalar(
                qs_t[:, qo + 1:qo + 2], qs_t[:, qo + 0:qo + 1], 1e-5, None,
                ALU.max), waits=[t1])
            t3 = p.emit("vector", lambda e: e.reciprocal(
                qs_t[:, qo + 2:qo + 3], qs_t[:, qo + 1:qo + 2]), waits=[t2])
            t4 = p.emit("vector", lambda e: e.tensor_scalar(
                qs_t[:, qo + 3:qo + 4], qs_t[:, qo + 2:qo + 3], 127.0, None,
                ALU.mult), waits=[t3])
            t5 = p.emit("vector", lambda e: e.tensor_scalar(
                qs_t[:, qo + 4:qo + 5], qs_t[:, qo + 1:qo + 2], 1.0 / 127.0, None,
                ALU.mult), waits=[t2])
            t6 = p.emit("vector", lambda e: e.tensor_scalar(
                qs_t[:, qo + 5:qo + 6], qs_t[:, qo + 4:qo + 5], scb[:, 6:7], None,
                ALU.mult), waits=[t5])
            t7 = p.emit("vector", lambda e: e.tensor_scalar(
                qs_t[:, qo + 6:qo + 7], qs_t[:, qo + 4:qo + 5], scb[:, 7:8], None,
                ALU.mult), waits=[t6])
            q1 = p.emit("vector", lambda e, b=b: e.tensor_scalar(
                h1c[b][:, :], xn2[b][:, :], qs_t[:, qo + 3:qo + 4], MAGIC,
                ALU.mult, ALU.add), waits=[t4])
            q2 = p.emit("vector", lambda e, b=b: e.tensor_scalar(
                xqb[b][:, :], h1c[b][:, :], MAGIC, None, ALU.subtract), waits=[q1])
            if "xq" in dbg:
                q2 = p.dma("scalar", dbg["xq"][tt * 128:(tt + 1) * 128, :],
                           xqb[b][:, :], waits=[q2], inc=sm["lddbg"])
            c1_buf[b] = q2
            c1_pre_toks[tt] = (q2, t6, t7)

        def c1_tr(tt, prev_ev):
            b = tt % 2
            q2 = c1_pre_toks[tt][0]
            trs = None
            for fcx in range(FC):
                trs = p.emit("tensor", lambda e, b=b, fcx=fcx: e.transpose(
                    ps_trb[:, fcx * 128:(fcx + 1) * 128],
                    xqb[b][:, fcx * 128:(fcx + 1) * 128], identb[:]),
                    waits=[q2] + ([prev_ev] if prev_ev else []))
            return p.emit("vector", lambda e, b=b: e.tensor_copy(
                xqT[b][:, :, :].rearrange("p f c -> p (f c)"), ps_trb[:, 0:1024]),
                waits=[trs])

        def c1_main(tt, xqT_ev, prev_cmb):
            b = tt % 2
            qo = 8 * b
            _, t6, t7 = c1_pre_toks[tt]
            hev = []
            for iw in range(8):
                mmg = None
                for fc in range(FC):
                    w0 = []
                    if fc == 0:
                        w0 = [xqT_ev]
                        if hev:
                            w0.append(hev[-1])
                        elif prev_cmb:
                            w0.append(prev_cmb)
                    mmg = p.emit("tensor", lambda e, iw=iw, fc=fc, b=b: e.matmul(
                        ps_a[:, 0:512], xqT[b][:, fc, :],
                        wgTq[:, fc, iw * 512:(iw + 1) * 512],
                        start=(fc == 0), stop=(fc == FC - 1)), waits=w0)
                mmu = None
                for fc in range(FC):
                    mmu = p.emit("tensor", lambda e, iw=iw, fc=fc, b=b: e.matmul(
                        ps_b[:, 0:512], xqT[b][:, fc, :],
                        wuTq[:, fc, iw * 512:(iw + 1) * 512],
                        start=(fc == 0), stop=(fc == FC - 1)),
                        waits=[mmg] if fc == 0 else [])
                sil = p.emit("scalar", lambda e, iw=iw: e.activation(
                    hsil[:, iw * 512:(iw + 1) * 512], ps_a[:, 0:512], AF.Silu,
                    scale=qs_t[:, qo + 5:qo + 6]), waits=[mmg, t6])
                cmb = p.emit("vector", lambda e, iw=iw: e.scalar_tensor_tensor(
                    hsil[:, iw * 512:(iw + 1) * 512], ps_b[:, 0:512],
                    qs_t[:, qo + 6:qo + 7], hsil[:, iw * 512:(iw + 1) * 512],
                    ALU.mult, ALU.mult), waits=[mmu, sil, t7])
                hev.append(cmb)
            c1_cmb_last[tt] = hev[-1]
            r1 = p.emit("vector", lambda e: e.reduce_max(
                qs_t[:, qo + 0:qo + 1], hsil[:, :], axis=AX.X,
                apply_absolute_value=True), waits=hev)
            r2 = p.emit("vector", lambda e: e.tensor_scalar(
                qs_t[:, qo + 1:qo + 2], qs_t[:, qo + 0:qo + 1], 1e-5, None,
                ALU.max), waits=[r1])
            r3 = p.emit("vector", lambda e: e.reciprocal(
                qs_t[:, qo + 2:qo + 3], qs_t[:, qo + 1:qo + 2]), waits=[r2])
            r4 = p.emit("vector", lambda e: e.tensor_scalar(
                qs_t[:, qo + 3:qo + 4], qs_t[:, qo + 2:qo + 3], 127.0, None,
                ALU.mult), waits=[r3])
            r5 = p.emit("vector", lambda e, tt=tt: e.tensor_scalar(
                rs2[:, tt:tt + 1], qs_t[:, qo + 1:qo + 2], 1.0 / 127.0, None,
                ALU.mult), waits=[r2])
            hq1 = p.emit("vector", lambda e: e.tensor_scalar(
                htmp[:, :], hsil[:, :], qs_t[:, qo + 3:qo + 4], MAGIC,
                ALU.mult, ALU.add), waits=[r4])
            hq2 = p.emit("vector", lambda e: e.tensor_scalar(
                hqt[:, :], htmp[:, :], MAGIC, None, ALU.subtract),
                waits=[hq1, r5] + ([c1_last] if c1_last else []))
            st = p.dma("sync", hq_sp[tt, :, :], hqt[:, :], waits=[hq2],
                       inc=sm["sthq"])
            if "hq" in dbg:
                st = p.dma("scalar", dbg["hq"][tt], hqt[:, :], waits=[hq2, st],
                           inc=sm["lddbg"])
            return st

        c1_pre(0, b2_done)
        ev = c1_tr(0, None)
        for tt in range(TT):
            if tt + 1 < TT:
                c1_pre(tt + 1, b2_done if tt + 1 < 2 else [])
            st = c1_main(tt, ev, c1_cmb_last.get(tt - 1))
            c1_last = st
            if tt + 1 < TT:
                ev = c1_tr(tt + 1, ev)

        c1_done = p.barrier(dma_sems=[sm["sthq"], sm["ldh0"], sm["ldh1"],
                                      sm["lddbg"]])

        lwd = p.dma("sync", wdTq[:, :, :], wd_sp.rearrange("a p d -> p a d"),
                    waits=c1_done, inc=sm["ldwd"])
        c2_buf = {}
        c2_out = {}

        def c2_pre(tt, extra):
            b = tt % 2
            lhq = p.dma("sync", hqb[b][:, :], hq_sp[tt, :, :],
                        waits=list(extra) + ([c2_buf.get(("hq", b))] if c2_buf.get(("hq", b)) else []),
                        inc=sm["ldr0" if b == 0 else "ldr1"])
            lh1 = p.dma("sync", h1r[b][:, :], h1_sp[tt * 128:(tt + 1) * 128, :],
                        waits=list(extra) + ([c2_buf.get(("h1", b))] if c2_buf.get(("h1", b)) else []),
                        inc=sm["ldh0" if b == 0 else "ldh1"])
            f3 = p.emit("vector", lambda e, tt=tt, b=b: e.tensor_scalar(
                qs_t[:, 14 + b:15 + b], rs2[:, tt:tt + 1], scb[:, 2:3], None,
                ALU.mult), waits=[lh1])
            return lhq, lh1, f3

        def c2_tr(tt, lhq, prev_ev):
            b = tt % 2
            tr_ev = []
            for ig in range(IC // 8):
                trs = None
                for j in range(8):
                    icx = ig * 8 + j
                    trs = p.emit("tensor", lambda e, b=b, icx=icx, j=j: e.transpose(
                        ps_trb[:, j * 128:(j + 1) * 128],
                        hqb[b][:, icx * 128:(icx + 1) * 128], identb[:]),
                        waits=[lwd, lhq] + ([tr_ev[-1]] if tr_ev else
                                            ([prev_ev] if prev_ev else [])))
                ev = p.emit("vector", lambda e, b=b, ig=ig: e.tensor_copy(
                    hqTs[b][:, ig * 8:(ig + 1) * 8, :].rearrange("p a c -> p (a c)"),
                    ps_trb[:, 0:1024]), waits=[trs])
                tr_ev.append(ev)
            return tr_ev[-1]

        def c2_main(tt, tr_tok, lh1, f3):
            b = tt % 2
            ev_out = []
            mm = None
            for mw in range(2):
                for ic in range(IC):
                    w0 = []
                    if ic == 0:
                        w0 = [tr_tok, lwd]
                        if ev_out:
                            w0.append(ev_out[-1])
                        elif c2_out.get("ev"):
                            w0.append(c2_out["ev"])
                    mm = p.emit("tensor", lambda e, ic=ic, mw=mw, b=b: e.matmul(
                        ps_a[:, 0:512], hqTs[b][:, ic, :],
                        wdTq[:, ic, mw * 512:(mw + 1) * 512],
                        start=(ic == 0), stop=(ic == IC - 1)), waits=w0)
                ev = p.emit("vector", lambda e, mw=mw, b=b: e.scalar_tensor_tensor(
                    outsb[b][:, mw * 512:(mw + 1) * 512], ps_a[:, 0:512],
                    qs_t[:, 14 + b:15 + b], h1r[b][:, mw * 512:(mw + 1) * 512],
                    ALU.mult, ALU.add),
                    waits=[mm, lh1, f3] + ([c2_out.get(("ob", b))] if c2_out.get(("ob", b)) else []))
                ev_out.append(ev)
            st = p.dma("sync", out_h[tt * 128:(tt + 1) * 128, :], outsb[b][:, :],
                       waits=[ev_out[-1]], inc=sm["stout0" if b == 0 else "stout1"])
            c2_buf[("hq", b)] = mm
            c2_buf[("h1", b)] = ev_out[-1]
            c2_out[("ob", b)] = st
            c2_out["ev"] = ev_out[-1]
            c2_out["mm"] = mm
            return st

        pres = {0: c2_pre(0, c1_done)}
        prev_tr = c2_tr(0, pres[0][0], None)
        for tt in range(TT):
            if tt + 1 < TT:
                pres[tt + 1] = c2_pre(tt + 1, c1_done if tt + 1 < 2 else [])
            st = c2_main(tt, prev_tr, pres[tt][1], pres[tt][2])
            if tt + 1 < TT:
                prev_tr = c2_tr(tt + 1, pres[tt + 1][0], c2_out.get("ev"))

        fin = p.barrier(dma_sems=[sm["stout0"], sm["stout1"], sm["lddbg"],
                                  sm["ldr0"], sm["ldr1"], sm["ldh0"], sm["ldh1"],
                                  sm["ldwd"]])
        p.emit("sync", lambda e: e.nop(), waits=fin, inc=None)

        block = es.enter_context(nc.Block())
        p.build(block)
    return nc, dbg


_NSEQ_PER_CORE = 2
_S = 2048
_NCORES = 8
_cache = {}


def _get_nc():
    if "nc" not in _cache:
        _cache["nc"] = build(nseq=_NSEQ_PER_CORE, S=_S)[0]
    return _cache["nc"]


def kernel(L0_attn_x, L0_attn_w_qkv, L0_attn_w_o, L0_ffn_w_gate, L0_ffn_w_up,
           L0_ffn_w_down):
    from concourse.bass_utils import run_bass_kernel_spmd
    x = np.ascontiguousarray(np.asarray(L0_attn_x, dtype=np.float32))
    wqkv = np.ascontiguousarray(
        np.asarray(L0_attn_w_qkv, dtype=np.float32).reshape(3 * NH * DK, D))
    wo = np.ascontiguousarray(np.asarray(L0_attn_w_o, dtype=np.float32))
    wg = np.ascontiguousarray(np.asarray(L0_ffn_w_gate, dtype=np.float32))
    wu = np.ascontiguousarray(np.asarray(L0_ffn_w_up, dtype=np.float32))
    wd = np.ascontiguousarray(np.asarray(L0_ffn_w_down, dtype=np.float32))

    B, S, _ = x.shape
    assert S == _S and B == _NCORES * _NSEQ_PER_CORE
    nc = _get_nc()
    in_maps = []
    for c in range(_NCORES):
        xs = x[c * _NSEQ_PER_CORE:(c + 1) * _NSEQ_PER_CORE].reshape(-1, D)
        in_maps.append({"x": np.ascontiguousarray(xs), "w_qkv": wqkv, "w_o": wo,
                        "w_gate": wg, "w_up": wu, "w_down": wd})
    res = run_bass_kernel_spmd(nc, in_maps, core_ids=list(range(_NCORES)))
    out = np.concatenate([np.asarray(r["out"], dtype=np.float32)
                          for r in res.results], axis=0)
    return out.reshape(B, S, D)
